# version 01 (final)
# baseline (speedup 1.0000x reference)
import numpy as np
import contextlib
import ml_dtypes
import concourse.bass as bass
import concourse.mybir as mybir
from concourse.bass_utils import run_bass_kernel_spmd

F32 = mybir.dt.float32
BF16 = mybir.dt.bfloat16
AF = mybir.ActivationFunctionType
ALU = mybir.AluOpType
AX = mybir.AxisListType
NEG = -1e37

def host_consts():
    c = {}
    c["i128b"] = np.eye(128, dtype=np.float32).astype(ml_dtypes.bfloat16)
    c["j128b"] = np.fliplr(np.eye(128)).astype(ml_dtypes.bfloat16)
    kc = np.arange(128)[:, None]
    qc = np.arange(512)[None, :]
    am = np.zeros((4, 128, 512), np.float32)
    for r in range(4):
        am[r] = np.where(qc >= kc + r * 128, 0.0, NEG)
    c["amask"] = am.reshape(512, 512).astype(ml_dtypes.bfloat16)
    p = np.arange(128)[:, None]; cc = np.arange(640)[None, :]
    c["fmask"] = np.where((cc >= p) & (cc <= p + 512), 0.0, NEG).astype(ml_dtypes.bfloat16)
    qq = np.arange(640)[None, :]
    c["smask"] = np.where((qq >= kc) & (qq <= kc + 512), 0.0, NEG).astype(ml_dtypes.bfloat16)
    DH = 64; W = 512; n = 2048
    half = np.arange(0, DH, 2, dtype=np.float64)
    inv_freq = 1.0 / (10000.0 ** (half / DH))
    t = np.arange(n, dtype=np.float64)
    freqs = t[:, None] * inv_freq[None, :]
    freqs = np.concatenate([freqs, freqs], 1)
    base = (half + 0.4 * DH) / (1.4 * DH)
    power = (t - n // 2) / (W // 2)
    scale = base[None, :] ** power[:, None]
    scale = np.concatenate([scale, scale], 1)
    cos = np.cos(freqs); sin = np.sin(freqs)
    sgn = np.where(np.arange(DH) < DH // 2, -1.0, 1.0)[None, :]
    c["csq"] = np.tile(cos * scale, (1, 8)).astype(np.float32)
    c["ssq"] = np.tile(sgn * sin * scale, (1, 8)).astype(np.float32)
    c["csk"] = np.tile(cos / scale, (1, 8)).astype(np.float32)
    c["ssk"] = np.tile(sgn * sin / scale, (1, 8)).astype(np.float32)
    return c

def host_weights(inp):
    w = {}
    qkv_w = np.asarray(inp["L0_attn_a_qkv_w"], np.float32)
    qkv_b = np.asarray(inp["L0_attn_a_qkv_b"], np.float32)
    w["a_qw"] = np.ascontiguousarray(qkv_w[:, :, 0:64].reshape(1024, 1024))
    w["a_kw"] = np.ascontiguousarray(qkv_w[:, :, 64:128].reshape(1024, 1024))
    w["a_vw"] = np.ascontiguousarray(qkv_w[:, :, 128:192].reshape(1024, 1024))
    w["a_qb"] = np.ascontiguousarray(qkv_b[:, 0:64].reshape(-1).reshape(8, 128).T)
    w["a_kb"] = np.ascontiguousarray(qkv_b[:, 64:128].reshape(-1).reshape(8, 128).T)
    w["a_vb"] = np.ascontiguousarray(qkv_b[:, 128:192].reshape(1, 1024))
    w["a_ow"] = np.ascontiguousarray(np.asarray(inp["L0_attn_a_out_w"], np.float32).reshape(1024, 1024))
    w["a_ob"] = np.ascontiguousarray(np.asarray(inp["L0_attn_a_out_b"], np.float32).reshape(1, 1024))
    for d in ("f", "b"):
        w[f"g_wz_{d}"] = np.ascontiguousarray(np.asarray(inp[f"L0_attn_b_Wz_{d}"], np.float32))
        w[f"g_wh_{d}"] = np.ascontiguousarray(np.asarray(inp[f"L0_attn_b_Wh_{d}"], np.float32))
        w[f"g_bz_{d}"] = np.ascontiguousarray(np.asarray(inp[f"L0_attn_b_bz_{d}"], np.float32).reshape(8, 128).T)
        w[f"g_bh_{d}"] = np.ascontiguousarray(np.asarray(inp[f"L0_attn_b_bh_{d}"], np.float32).reshape(8, 128).T)
    w["g_wo"] = np.ascontiguousarray(np.asarray(inp["L0_attn_b_W_out"], np.float32))
    w["g_bo"] = np.ascontiguousarray(np.asarray(inp["L0_attn_b_b_out"], np.float32).reshape(1, 1024))
    HID = 2730; HP = 2816
    for l in range(2):
        w[f"f_wqkv{l}"] = np.ascontiguousarray(np.asarray(inp["L0_ffn_wqkv"][l], np.float32))
        w[f"f_wo{l}"] = np.ascontiguousarray(np.asarray(inp["L0_ffn_wo"][l], np.float32))
        w1 = np.asarray(inp["L0_ffn_w1"][l], np.float32)
        w1a = np.zeros((1024, HP), np.float32); w1a[:, :HID] = w1[:, :HID]
        w1g = np.zeros((1024, HP), np.float32); w1g[:, :HID] = w1[:, HID:]
        w[f"f_w1a{l}"] = w1a; w[f"f_w1g{l}"] = w1g
        b1 = np.asarray(inp["L0_ffn_b1"][l], np.float32)
        b1a = np.zeros((1, HP), np.float32); b1a[0, :HID] = b1[:HID]
        b1g = np.zeros((1, HP), np.float32); b1g[0, :HID] = b1[HID:]
        w[f"f_b1a{l}"] = b1a; w[f"f_b1g{l}"] = b1g
        w2 = np.zeros((HP, 1024), np.float32); w2[:HID] = np.asarray(inp["L0_ffn_w2"][l], np.float32)
        w[f"f_w2{l}"] = w2
        w[f"f_b2{l}"] = np.ascontiguousarray(np.asarray(inp["L0_ffn_b2"][l], np.float32).reshape(1, 1024))
        w[f"f_lg{l}"] = np.ascontiguousarray(np.asarray(inp["L0_ffn_ln_g"][l], np.float32).reshape(1, 1024))
        w[f"f_lbc{l}"] = np.ascontiguousarray(np.asarray(inp["L0_ffn_ln_b"][l], np.float32).reshape(8, 128).T)
        w[f"f_qs{l}"] = np.ascontiguousarray(np.asarray(inp["L0_ffn_q_scale"][l], np.float32).reshape(1, 64))
        w[f"f_ks{l}"] = np.ascontiguousarray(np.asarray(inp["L0_ffn_k_scale"][l], np.float32).reshape(1, 64))
    return w

class Buf:
    __slots__ = ("w", "rs")
    def __init__(self):
        self.w = None; self.rs = []

class T:
    __slots__ = ("t", "b")
    def __init__(self, t, b=None):
        self.t = t; self.b = b if b is not None else Buf()
    def __getitem__(self, key):
        return self.t[key]
    def view(self):
        return self.t

class Ctx:
    def __init__(self, nc, st):
        self.nc = nc
        self.eng = {"pe": nc.tensor, "act": nc.scalar, "dve": nc.vector,
                    "pool": nc.gpsimd, "sync": nc.sync}
        self.streams = ["pe", "act", "dve", "pool", "sync", "dma_sync", "dma_act", "dma_pool"]
        self.sem = {k: st.enter_context(nc.semaphore("sem_" + k)) for k in self.streams}
        self.cnt = {k: 0 for k in self.streams}
        self.hw = {}
        self.prog = {k: [] for k in self.eng}
        self.nalloc = 0

    def emit(self, eng, fn, reads=(), writes=(), dma=False):
        deps = []
        for x in reads:
            b = x.b if isinstance(x, T) else x
            if b.w is not None: deps.append(b.w)
        wbs = []
        for x in writes:
            b = x.b if isinstance(x, T) else x
            wbs.append(b)
            if b.w is not None: deps.append(b.w)
            deps.extend(b.rs)
        need = {}
        if dma:
            q0 = "dma_" + eng
            if self.cnt[q0]:
                need[q0] = self.cnt[q0]
        for s, v in deps:
            if s == eng == "pe":
                continue
            if need.get(s, 0) < v: need[s] = v
        waits = []
        for s, v in need.items():
            if self.hw.get((eng, s), 0) < v:
                self.hw[(eng, s)] = v
                waits.append((s, v))
        if dma:
            q = "dma_" + eng
            self.cnt[q] += 16
            tok = (q, self.cnt[q])
            isem, inc = self.sem[q], 16
        else:
            self.cnt[eng] += 1
            tok = (eng, self.cnt[eng])
            isem, inc = self.sem[eng], 1
        sem = self.sem
        def closure(e, fn=fn, waits=waits, isem=isem, inc=inc):
            for s, v in waits:
                e.wait_ge(sem[s], v)
            fn(e).then_inc(isem, inc)
        self.prog[eng].append(closure)
        for x in reads:
            b = x.b if isinstance(x, T) else x
            b.rs.append(tok)
        for b in wbs:
            b.w = tok; b.rs = []
        return tok

    def barrier(self):
        snap = dict(self.cnt)
        sem = self.sem
        for eng in self.eng:
            waits = []
            for s in self.streams:
                if s == eng: continue
                v = snap[s]
                if v and self.hw.get((eng, s), 0) < v:
                    self.hw[(eng, s)] = v
                    waits.append((s, v))
            if waits:
                def closure(e, waits=waits):
                    for s, v in waits:
                        e.wait_ge(sem[s], v)
                self.prog[eng].append(closure)

    def sbuf(self, stack, shape, dt=F32):
        self.nalloc += 1
        t = stack.enter_context(self.nc.sbuf_tensor(f"sb{self.nalloc}", shape, dt))
        return T(t)

    def psum(self, stack, shape, dt=F32):
        self.nalloc += 1
        t = stack.enter_context(self.nc.psum_tensor(f"ps{self.nalloc}", shape, dt))
        return T(t)

    def finalize(self):
        self.barrier()
        nc = self.nc
        with nc.Block() as block:
            @block.sync
            def _(e):
                for f in self.prog["sync"]: f(e)
            @block.scalar
            def _(e):
                for f in self.prog["act"]: f(e)
            @block.vector
            def _(e):
                for f in self.prog["dve"]: f(e)
            @block.tensor
            def _(e):
                for f in self.prog["pe"]: f(e)
            @block.gpsimd
            def _(e):
                for f in self.prog["pool"]: f(e)

def build(nc, dbg=None):
    gst = contextlib.ExitStack()
    ctx = Ctx(nc, gst)
    E = ctx.emit

    xin = nc.dram_tensor("xin", [4096, 1024], F32, kind="ExternalInput").ap()
    outp = nc.dram_tensor("out", [4096, 1024], F32, kind="ExternalOutput").ap()
    SH = {"i128b": [128, 128], "j128b": [128, 128],
          "amask": [512, 512], "fmask": [128, 640], "smask": [128, 640],
          "csq": [2048, 512], "ssq": [2048, 512], "csk": [2048, 512], "ssk": [2048, 512],
          "a_qw": [1024, 1024], "a_kw": [1024, 1024], "a_vw": [1024, 1024],
          "a_qb": [128, 8], "a_kb": [128, 8], "a_vb": [1, 1024],
          "a_ow": [1024, 1024], "a_ob": [1, 1024],
          "g_wo": [2048, 1024], "g_bo": [1, 1024]}
    for d in ("f", "b"):
        SH[f"g_wz_{d}"] = [1024, 1024]; SH[f"g_wh_{d}"] = [1024, 1024]
        SH[f"g_bz_{d}"] = [128, 8]; SH[f"g_bh_{d}"] = [128, 8]
    for l in range(2):
        SH[f"f_wqkv{l}"] = [1024, 1536]; SH[f"f_wo{l}"] = [512, 1024]
        SH[f"f_w1a{l}"] = [1024, 2816]; SH[f"f_w1g{l}"] = [1024, 2816]
        SH[f"f_b1a{l}"] = [1, 2816]; SH[f"f_b1g{l}"] = [1, 2816]
        SH[f"f_w2{l}"] = [2816, 1024]; SH[f"f_b2{l}"] = [1, 1024]
        SH[f"f_lg{l}"] = [1, 1024]; SH[f"f_lbc{l}"] = [128, 8]
        SH[f"f_qs{l}"] = [1, 64]; SH[f"f_ks{l}"] = [1, 64]
    BF_CONSTS = ("i128b", "j128b", "amask", "fmask", "smask")
    D = {k: nc.dram_tensor(k, v, BF16 if k in BF_CONSTS else F32, kind="ExternalInput").ap()
         for k, v in SH.items()}
    dbuf = {k: Buf() for k in list(SH) + ["xin", "xd0", "xd1", "yd", "outp"]}
    xd0 = nc.dram_tensor("xd0", [2048, 1024], F32, kind="Internal").ap()
    xd1 = nc.dram_tensor("xd1", [2048, 1024], F32, kind="Internal").ap()
    yd = nc.dram_tensor("yd", [2048, 1024], F32, kind="Internal").ap()
    XD = {0: xd0, 1: xd1}
    WBKEYS = (["a_qw", "a_kw", "a_vw", "a_ow", "g_wz_f", "g_wh_f", "g_wz_b", "g_wh_b", "g_wo"] +
              [f"f_wqkv{l}" for l in range(2)] + [f"f_wo{l}" for l in range(2)] +
              [f"f_w1a{l}" for l in range(2)] + [f"f_w1g{l}" for l in range(2)] +
              [f"f_w2{l}" for l in range(2)])
    WB = {}
    for k in WBKEYS:
        WB[k] = nc.dram_tensor("wb_" + k, SH[k], BF16, kind="Internal").ap()
        dbuf["wb_" + k] = Buf()

    i128b = ctx.sbuf(gst, [128, 128], BF16)
    j128b = ctx.sbuf(gst, [128, 128], BF16)
    onesb = ctx.sbuf(gst, [1, 128], BF16)
    onesf = ctx.sbuf(gst, [1, 128], F32)
    one1f = ctx.sbuf(gst, [1, 1], F32)
    amask = ctx.sbuf(gst, [128, 2048], BF16)
    fmask = ctx.sbuf(gst, [128, 640], BF16)
    smask = ctx.sbuf(gst, [128, 640], BF16)
    aqb = ctx.sbuf(gst, [128, 8]); akb = ctx.sbuf(gst, [128, 8])
    avb = ctx.sbuf(gst, [1, 1024], BF16); aob = ctx.sbuf(gst, [1, 1024], BF16)
    gbz = {d: ctx.sbuf(gst, [128, 8]) for d in ("f", "b")}
    gbzn = {d: ctx.sbuf(gst, [128, 8]) for d in ("f", "b")}
    gbh = {d: ctx.sbuf(gst, [128, 8]) for d in ("f", "b")}
    gbo = ctx.sbuf(gst, [1, 1024], BF16)
    fb2 = [ctx.sbuf(gst, [1, 1024], BF16) for _ in range(2)]
    b1ea = [ctx.sbuf(gst, [128, 22]) for _ in range(2)]
    b1eg = [ctx.sbuf(gst, [128, 22]) for _ in range(2)]
    lgb = [ctx.sbuf(gst, [128, 1024], BF16) for _ in range(2)]
    qsv = [ctx.sbuf(gst, [128, 64]) for _ in range(2)]
    ksv = [ctx.sbuf(gst, [128, 64]) for _ in range(2)]
    lbcb = [ctx.sbuf(gst, [128, 8], BF16) for _ in range(2)]
    xt0 = ctx.sbuf(gst, [128, 1024])
    xt1 = ctx.sbuf(gst, [128, 1024])
    yt0 = ctx.sbuf(gst, [128, 1024])
    lnt = ctx.sbuf(gst, [128, 1024], BF16)
    bnS = ctx.sbuf(gst, [128, 12]); bnM = ctx.sbuf(gst, [128, 2])
    rstd = ctx.sbuf(gst, [128, 1]); nmr = ctx.sbuf(gst, [128, 1])
    mcol = ctx.sbuf(gst, [128, 1], BF16); mcol2 = ctx.sbuf(gst, [128, 1], BF16)
    epsb = ctx.sbuf(gst, [128, 1])
    rrow = ctx.sbuf(gst, [1, 512])
    asb = ctx.sbuf(gst, [128, 512], BF16)
    psT = ctx.psum(gst, [128, 128])
    psBC = ctx.psum(gst, [128, 512])
    psA = ctx.psum(gst, [128, 512])
    psB = ctx.psum(gst, [128, 512])
    psAT = ctx.psum(gst, [128, 2048])

    def ln_tile(xt):
        E("dve", lambda e: e.bn_stats(out=bnS[:, 0:6], in_=xt[:, 0:512]), reads=(xt,), writes=(bnS,))
        E("dve", lambda e: e.bn_stats(out=bnS[:, 6:12], in_=xt[:, 512:1024]), reads=(xt,), writes=(bnS,))
        E("dve", lambda e: e.bn_aggr(out=bnM[:, :], in_=bnS[:, :]), reads=(bnS,), writes=(bnM,))
        E("act", lambda e: e.activation(out=rstd[:, :], in_=bnM[:, 1:2], func=AF.Sqrt, bias=epsb[:, 0:1]),
          reads=(bnM, epsb), writes=(rstd,))
        E("dve", lambda e: e.reciprocal(out=rstd[:, :], in_=rstd[:, :]), reads=(rstd,), writes=(rstd,))
        E("dve", lambda e: e.scalar_tensor_tensor(out=nmr[:, :], in0=bnM[:, 0:1], scalar=-1.0,
                                                  in1=rstd[:, :], op0=ALU.mult, op1=ALU.mult),
          reads=(bnM, rstd), writes=(nmr,))

    def normed(dst, xt, dst_slice=None):
        out_ap = dst[:, :] if dst_slice is None else dst_slice
        E("act", lambda e: e.activation(out=out_ap, in_=xt[:, :], func=AF.Identity,
                                        bias=nmr[:, 0:1], scale=rstd[:, 0:1]),
          reads=(xt, rstd, nmr), writes=(dst,))

    def mm(out_ap, lhsT, lhs_ap, rhs, rhs_ap, psb, start, stop):
        E("pe", lambda e: e.matmul(out_ap, lhs_ap, rhs_ap, start=start, stop=stop),
          reads=(lhsT, rhs), writes=(psb,))

    def dma_in(dst, dst_ap, key, src_ap):
        E("sync", lambda e: e.dma_start(out=dst_ap, in_=src_ap),
          reads=(dbuf[key],), writes=(dst,), dma=True)

    def dma_out(eng, key, dst_ap, src, src_ap):
        E(eng, lambda e: e.dma_start(out=dst_ap, in_=src_ap),
          reads=(src,), writes=(dbuf[key],), dma=True)

    def transpose_block(dst, dst_ap, src, src_ap, ident):
        E("pe", lambda e: e.matmul(psT[:, :], src_ap, ident[:, :], start=True, stop=True),
          reads=(src, ident), writes=(psT,))
        E("act", lambda e: e.copy(out=dst_ap, in_=psT[:, :]), reads=(psT,), writes=(dst,))

    with contextlib.ExitStack() as pst:
        stage = ctx.sbuf(pst, [128, 2816])
        stageb = ctx.sbuf(pst, [128, 2816], BF16)
        dma_in(fmask, fmask[:, :], "fmask", D["fmask"][:, :])
        dma_in(smask, smask[:, :], "smask", D["smask"][:, :])
        dma_in(amask, amask[:, :].rearrange("p (r c) -> p r c", r=4),
               "amask", D["amask"][:, :].rearrange("(r p) c -> p r c", p=128))
        dma_in(aqb, aqb[:, :], "a_qb", D["a_qb"][:, :])
        dma_in(akb, akb[:, :], "a_kb", D["a_kb"][:, :])
        for d in ("f", "b"):
            dma_in(gbz[d], gbz[d][:, :], f"g_bz_{d}", D[f"g_bz_{d}"][:, :])
            dma_in(gbh[d], gbh[d][:, :], f"g_bh_{d}", D[f"g_bh_{d}"][:, :])
            E("dve", lambda e, d=d: e.tensor_scalar_mul(gbzn[d][:, :], gbz[d][:, :], -1.0),
              reads=(gbz[d],), writes=(gbzn[d],))
        E("dve", lambda e: e.memset(onesb[:, :], 1.0), writes=(onesb,))
        E("dve", lambda e: e.memset(epsb[:, :], 1e-5), writes=(epsb,))
        E("dve", lambda e: e.memset(onesf[:, :], 1.0), writes=(onesf,))
        E("dve", lambda e: e.memset(one1f[:, :], 1.0), writes=(one1f,))
        dma_in(i128b, i128b[:, :], "i128b", D["i128b"][:, :])
        dma_in(j128b, j128b[:, :], "j128b", D["j128b"][:, :])
        def castf(dst, key, rows, cols):
            dma_in(stage, stage[0:rows, 0:cols], key, D[key][:, :])
            E("act", lambda e, dst=dst, rows=rows, cols=cols:
              e.copy(out=dst[0:rows, 0:cols], in_=stage[0:rows, 0:cols]),
              reads=(stage,), writes=(dst,))
        castf(avb, "a_vb", 1, 1024)
        castf(aob, "a_ob", 1, 1024)
        castf(gbo, "g_bo", 1, 1024)
        for l in range(2):
            castf(fb2[l], f"f_b2{l}", 1, 1024)
            castf(lbcb[l], f"f_lbc{l}", 128, 8)
        def bcast(dst, key, n):
            dma_in(stage, stage[0:1, 0:n], key, D[key][:, :])
            for c0 in range(0, n, 512):
                w = min(512, n - c0)
                E("pe", lambda e, c0=c0, w=w: e.matmul(psBC[:, 0:w], onesf[:, :], stage[0:1, c0:c0 + w],
                                                       start=True, stop=True),
                  reads=(stage, onesf), writes=(psBC,))
                E("act", lambda e, c0=c0, w=w: e.copy(out=dst[:, c0:c0 + w], in_=psBC[:, 0:w]),
                  reads=(psBC,), writes=(dst,))
        for l in range(2):
            bcast(lgb[l], f"f_lg{l}", 1024)
            bcast(qsv[l], f"f_qs{l}", 64)
            bcast(ksv[l], f"f_ks{l}", 64)
        for key in WBKEYS:
            rows, cols = SH[key]
            for c in range(rows // 128):
                dma_in(stage, stage[:, 0:cols], key, D[key][c * 128:(c + 1) * 128, :])
                E("act", lambda e, cols=cols: e.copy(out=stageb[:, 0:cols], in_=stage[:, 0:cols]),
                  reads=(stage,), writes=(stageb,))
                dma_out("act", "wb_" + key, WB[key][c * 128:(c + 1) * 128, :], stageb, stageb[:, 0:cols])
        wband = ctx.sbuf(pst, [128, 4096], BF16)
        rowres = ctx.sbuf(pst, [1, 2816])
        for l in range(2):
            for ag, dst in (("a", b1ea[l]), ("g", b1eg[l])):
                key = f"f_w1{ag}{l}"
                for band0 in range(0, 2816, 512):
                    w = min(512, 2816 - band0)
                    E("sync", lambda e, key=key, band0=band0, w=w:
                      e.dma_start(out=wband[:, 0:8 * w],
                                  in_=WB[key][:, band0:band0 + w].rearrange("(c p) n -> p c n", p=128)),
                      reads=(dbuf["wb_" + key],), writes=(wband,), dma=True)
                    for c in range(8):
                        E("pe", lambda e, c=c, w=w, l=l: e.matmul(psBC[0:1, 0:w], lbcb[l][:, c:c + 1],
                                                                  wband[:, c * w:(c + 1) * w],
                                                                  start=(c == 0), stop=(c == 7)),
                          reads=(wband, lbcb[l]), writes=(psBC,))
                    dma_in(stage, stage[0:1, 0:w], f"f_b1{ag}{l}", D[f"f_b1{ag}{l}"][:, band0:band0 + w])
                    E("dve", lambda e, band0=band0, w=w:
                      e.tensor_tensor(out=rowres[0:1, band0:band0 + w], in0=psBC[0:1, 0:w],
                                      in1=stage[0:1, 0:w], op=ALU.add),
                      reads=(psBC, stage), writes=(rowres,))
                for hc in range(22):
                    E("pe", lambda e, hc=hc: e.matmul(psT[:, 0:1], rowres[0:1, hc * 128:(hc + 1) * 128],
                                                      one1f[:, :], start=True, stop=True),
                      reads=(rowres, one1f), writes=(psT,))
                    E("act", lambda e, hc=hc, dst=dst: e.copy(out=dst[:, hc:hc + 1], in_=psT[:, 0:1]),
                      reads=(psT,), writes=(dst,))
        ctx.barrier()

    def pieces(lo, hi, align=512):
        out = []
        c = lo
        while c < hi:
            nxt = min(hi, (c // align + 1) * align)
            out.append((c, nxt - c))
            c = nxt
        return out

    def phase_A(b0):
        row0 = b0 * 2048
        with contextlib.ExitStack() as ast:
            lnT = [ctx.sbuf(ast, [128, 1024], BF16) for _ in range(8)]
            QT = [ctx.sbuf(ast, [128, 2048], BF16) for _ in range(4)]
            KT = [ctx.sbuf(ast, [128, 2048], BF16) for _ in range(4)]
            VA = [ctx.sbuf(ast, [128, 520], BF16) for _ in range(16)]
            AT = [ctx.sbuf(ast, [128, 2048], BF16) for _ in range(4)]
            PTb = ctx.sbuf(ast, [128, 1024], BF16)
            wkt = ctx.sbuf(ast, [128, 8192], BF16)
            for hh in range(2):
                for th in range(2):
                    for tt in range(8):
                        t = th * 8 + tt
                        dma_in(xt0, xt0[:, :], "xin", xin[row0 + t * 128: row0 + (t + 1) * 128, :])
                        ln_tile(xt0)
                        normed(lnt, xt0)
                        for c in range(8):
                            transpose_block(lnT[c], lnT[c][:, tt * 128:(tt + 1) * 128],
                                            lnt, lnt[:, c * 128:(c + 1) * 128], i128b)
                    for wkey, dstl, bias in (("a_qw", QT, aqb), ("a_kw", KT, akb)):
                        for j in range(4):
                            gj = hh * 4 + j
                            E("sync", lambda e, wkey=wkey, gj=gj:
                              e.dma_start(out=wkt[:, 0:1024],
                                          in_=WB[wkey][:, gj * 128:(gj + 1) * 128]
                                          .rearrange("(c p) n -> p c n", p=128)),
                              reads=(dbuf["wb_" + wkey],), writes=(wkt,), dma=True)
                            for s in range(2):
                                for c in range(8):
                                    mm(psA[:, :], wkt, wkt[:, c * 128:(c + 1) * 128],
                                       lnT[c], lnT[c][:, s * 512:(s + 1) * 512],
                                       psA, c == 0, c == 7)
                                E("act", lambda e, j=j, s=s, th=th, gj=gj, dstl=dstl, bias=bias:
                                  e.activation(out=dstl[j][:, th * 1024 + s * 512: th * 1024 + (s + 1) * 512],
                                               in_=psA[:, :], func=AF.Identity, bias=bias[:, gj:gj + 1]),
                                  reads=(psA, bias), writes=(dstl[j],))
                    E("sync", lambda e, hh=hh:
                      e.dma_start(out=wkt[:, 0:4096],
                                  in_=WB["a_vw"][:, hh * 512:(hh + 1) * 512]
                                  .rearrange("(c p) n -> p c n", p=128)),
                      reads=(dbuf["wb_a_vw"],), writes=(wkt,), dma=True)
                    for tt in range(8):
                        t = th * 8 + tt
                        E("dve", lambda e, t=t:
                          e.memset(VA[t][:, :].rearrange("p (h v) -> p h v", v=65)[:, :, 64:65], 1.0),
                          writes=(VA[t],))
                        for c in range(8):
                            mm(psA[:, :], lnT[c], lnT[c][:, tt * 128:(tt + 1) * 128],
                               wkt, wkt[:, c * 512:(c + 1) * 512], psA, c == 0, False)
                        E("pe", lambda e, hh=hh: e.matmul(psA[:, :], onesb[:, :],
                                                          avb[0:1, hh * 512:(hh + 1) * 512],
                                                          start=False, stop=True),
                          reads=(onesb, avb), writes=(psA,))
                        E("act", lambda e, t=t:
                          e.activation(out=VA[t][:, :].rearrange("p (h v) -> p h v", v=65)[:, :, 0:64],
                                       in_=psA[:, :].rearrange("p (h d) -> p h d", d=64), func=AF.Copy),
                          reads=(psA,), writes=(VA[t],))
                for h in range(8):
                    jt = h // 2; r0 = (h % 2) * 64
                    for kt in range(16):
                        for band in range(kt // 4, 4):
                            ps = psA if band % 2 == 0 else psB
                            mm(ps[:, :], KT[jt], KT[jt][r0:r0 + 64, kt * 128:(kt + 1) * 128],
                               QT[jt], QT[jt][r0:r0 + 64, band * 512:(band + 1) * 512],
                               ps, True, True)
                            if band == kt // 4:
                                rr = kt % 4
                                E("dve", lambda e, ps=ps, rr=rr:
                                  e.tensor_tensor(out=ps[:, :], in0=ps[:, :],
                                                  in1=amask[:, rr * 512:(rr + 1) * 512], op=ALU.add),
                                  reads=(ps, amask), writes=(ps,))
                            E("act", lambda e, ps=ps, band=band:
                              e.activation(out=PTb[:, (band % 2) * 512:(band % 2 + 1) * 512],
                                           in_=ps[:, :], func=AF.Exp, scale=0.125),
                              reads=(ps,), writes=(PTb,))
                            E("pe", lambda e, kt=kt, band=band, h=h:
                              e.matmul(psAT[0:65, band * 512:(band + 1) * 512],
                                       VA[kt][:, h * 65:(h + 1) * 65],
                                       PTb[:, (band % 2) * 512:(band % 2 + 1) * 512],
                                       start=(kt == 0), stop=(kt == 4 * band + 3)),
                              reads=(VA[kt], PTb), writes=(psAT,))
                    for band in range(4):
                        E("dve", lambda e, band=band:
                          e.reciprocal(out=rrow[0:1, :], in_=psAT[64:65, band * 512:(band + 1) * 512]),
                          reads=(psAT,), writes=(rrow,))
                        E("act", lambda e, band=band:
                          e.copy(out=asb[0:64, :], in_=psAT[0:64, band * 512:(band + 1) * 512]),
                          reads=(psAT,), writes=(asb,))
                        E("pe", lambda e: e.matmul(psBC[:, :], onesf[:, :], rrow[0:1, :],
                                                   start=True, stop=True),
                          reads=(rrow, onesf), writes=(psBC,))
                        E("dve", lambda e, band=band, jt=jt, r0=r0:
                          e.tensor_tensor(out=AT[jt][r0:r0 + 64, band * 512:(band + 1) * 512],
                                          in0=asb[0:64, :], in1=psBC[0:64, :], op=ALU.mult),
                          reads=(asb, psBC), writes=(AT[jt],))
                E("sync", lambda e, hh=hh:
                  e.dma_start(out=wkt[:, 0:4096],
                              in_=WB["a_ow"][hh * 512:(hh + 1) * 512, :]
                              .rearrange("(c p) n -> p c n", p=128)),
                  reads=(dbuf["wb_a_ow"],), writes=(wkt,), dma=True)
                for t in range(16):
                    for half in range(2):
                        ps = psA if half == 0 else psB
                        for j in range(4):
                            mm(ps[:, :], AT[j], AT[j][:, t * 128:(t + 1) * 128],
                               wkt, wkt[:, j * 1024 + half * 512: j * 1024 + (half + 1) * 512],
                               ps, j == 0, hh == 1 and j == 3)
                        if hh == 0:
                            E("pe", lambda e, half=half, ps=ps:
                              e.matmul(ps[:, :], onesb[:, :], aob[0:1, half * 512:(half + 1) * 512],
                                       start=False, stop=True),
                              reads=(onesb, aob), writes=(ps,))
                    src_d, src_key = (xin, "xin") if hh == 0 else (XD[0], "xd0")
                    off = row0 if hh == 0 else 0
                    dma_in(xt1, xt1[:, :], src_key, src_d[off + t * 128: off + (t + 1) * 128, :])
                    E("dve", lambda e: e.tensor_tensor(out=xt1[:, 0:512], in0=xt1[:, 0:512],
                                                       in1=psA[:, :], op=ALU.add),
                      reads=(xt1, psA), writes=(xt1,))
                    E("dve", lambda e: e.tensor_tensor(out=xt1[:, 512:1024], in0=xt1[:, 512:1024],
                                                       in1=psB[:, :], op=ALU.add),
                      reads=(xt1, psB), writes=(xt1,))
                    dma_out("pool", "xd0", XD[0][t * 128:(t + 1) * 128, :], xt1, xt1[:, :])
                    if hh == 1:
                        ln_tile(xt1)
                        normed(yt0, xt1)
                        dma_out("act", "yd", yd[t * 128:(t + 1) * 128, :], yt0, yt0[:, :])
            ctx.barrier()

    def ffn_layer(l):
        with contextlib.ExitStack() as fst:
            yT = [ctx.sbuf(fst, [128, 1032], BF16) for _ in range(8)]
            QTa = [ctx.sbuf(fst, [128, 2048], BF16) for _ in range(8)]
            KTa = [ctx.sbuf(fst, [128, 2048], BF16) for _ in range(8)]
            VA = [ctx.sbuf(fst, [128, 520], BF16) for _ in range(16)]
            AT = [ctx.sbuf(fst, [128, 2048], BF16) for _ in range(4)]
            PTb = ctx.sbuf(fst, [128, 1536], BF16)
            wkt = ctx.sbuf(fst, [128, 12288], BF16)
            scr = ctx.sbuf(fst, [128, 2048])
            scr2 = ctx.sbuf(fst, [128, 1024])
            qrot = ctx.sbuf(fst, [128, 512], BF16)
            rott = ctx.sbuf(fst, [128, 1024])
            n2 = ctx.sbuf(fst, [128, 8]); rn = ctx.sbuf(fst, [128, 8])
            for h in range(8):
                E("dve", lambda e, h=h: e.memset(KTa[h][64:65, :], 1.0), writes=(KTa[h],))
            for j in range(8):
                E("sync", lambda e, j=j, l=l:
                  e.dma_start(out=wkt[:, j * 1536:(j + 1) * 1536],
                              in_=WB[f"f_wqkv{l}"][j * 128:(j + 1) * 128, :]),
                  reads=(dbuf[f"wb_f_wqkv{l}"],), writes=(wkt,), dma=True)
            for th in range(2):
                for c in range(8):
                    if th == 0:
                        E("dve", lambda e, c=c: e.memset(yT[c][:, 0:1], 0.0), writes=(yT[c],))
                    else:
                        E("dve", lambda e, c=c: e.tensor_copy(out=yT[c][:, 0:1], in_=yT[c][:, 1024:1025]),
                          reads=(yT[c],), writes=(yT[c],))
                for tt in range(8):
                    t = th * 8 + tt
                    dma_in(yt0, yt0[:, :], "yd", yd[t * 128:(t + 1) * 128, :])
                    E("act", lambda e: e.copy(out=lnt[:, :], in_=yt0[:, :]), reads=(yt0,), writes=(lnt,))
                    for c in range(8):
                        transpose_block(yT[c], yT[c][:, tt * 128 + 1: tt * 128 + 129],
                                        lnt, lnt[:, c * 128:(c + 1) * 128], i128b)
                for tt in range(8):
                    t = th * 8 + tt
                    for which in ("q", "k", "v"):
                        off = {"q": 0, "k": 512, "v": 1024}[which]
                        for c in range(8):
                            so = 1 if c < 4 else 0
                            mm(psA[:, :], yT[c], yT[c][:, tt * 128 + so: tt * 128 + so + 128],
                               wkt, wkt[:, c * 1536 + off: c * 1536 + off + 512],
                               psA, c == 0, c == 7)
                        if which == "v":
                            E("dve", lambda e, t=t:
                              e.memset(VA[t][:, :].rearrange("p (h v) -> p h v", v=65)[:, :, 64:65], 1.0),
                              writes=(VA[t],))
                            E("act", lambda e, t=t:
                              e.activation(out=VA[t][:, :].rearrange("p (h v) -> p h v", v=65)[:, :, 0:64],
                                           in_=psA[:, :].rearrange("p (h d) -> p h d", d=64),
                                           func=AF.Copy),
                              reads=(psA,), writes=(VA[t],))
                            continue
                        qs_ = scr[:, 0:512]; sq_ = scr[:, 512:1024]
                        qn_ = scr[:, 1024:1536]; qn2_ = scr[:, 1536:2048]
                        E("act", lambda e: e.copy(out=qs_, in_=psA[:, :]), reads=(psA,), writes=(scr,))
                        E("act", lambda e: e.activation(out=sq_, in_=qs_, func=AF.Square),
                          reads=(scr,), writes=(scr,))
                        E("dve", lambda e: e.reduce_sum(out=n2[:, :],
                                                        in_=sq_.rearrange("p (h d) -> p h d", d=64),
                                                        axis=AX.X),
                          reads=(scr,), writes=(n2,))
                        E("act", lambda e: e.activation(out=rn[:, :], in_=n2[:, :], func=AF.Sqrt),
                          reads=(n2,), writes=(rn,))
                        E("dve", lambda e: e.tensor_scalar_max(rn[:, :], rn[:, :], 1e-12),
                          reads=(rn,), writes=(rn,))
                        E("dve", lambda e: e.reciprocal(out=rn[:, :], in_=rn[:, :]),
                          reads=(rn,), writes=(rn,))
                        E("dve", lambda e: e.tensor_tensor(
                            out=qn_.rearrange("p (h d) -> p h d", d=64),
                            in0=qs_.rearrange("p (h d) -> p h d", d=64),
                            in1=rn[:, :].unsqueeze(2).broadcast_to([128, 8, 64]), op=ALU.mult),
                          reads=(scr, rn), writes=(scr,))
                        sv = qsv[l] if which == "q" else ksv[l]
                        E("dve", lambda e, sv=sv: e.tensor_tensor(
                            out=qn2_.rearrange("p (h d) -> p h d", d=64),
                            in0=qn_.rearrange("p (h d) -> p h d", d=64),
                            in1=sv[:, :].unsqueeze(1).broadcast_to([128, 8, 64]), op=ALU.mult),
                          reads=(scr, sv), writes=(scr,))
                        cs_key = "csq" if which == "q" else "csk"
                        ss_key = "ssq" if which == "q" else "ssk"
                        dma_in(rott, rott[:, 0:512], cs_key, D[cs_key][t * 128:(t + 1) * 128, :])
                        dma_in(rott, rott[:, 512:1024], ss_key, D[ss_key][t * 128:(t + 1) * 128, :])
                        t1_ = scr2[:, 0:512]; t2_ = scr2[:, 512:1024]
                        E("pool", lambda e: e.tensor_tensor(out=t1_, in0=qn2_, in1=rott[:, 0:512],
                                                            op=ALU.mult),
                          reads=(scr, rott), writes=(scr2,))
                        E("pool", lambda e: e.tensor_tensor(
                            out=t2_.rearrange("p (h s d) -> p h s d", s=2, d=32)[:, :, 0, :],
                            in0=qn2_.rearrange("p (h s d) -> p h s d", s=2, d=32)[:, :, 1, :],
                            in1=rott[:, 512:1024].rearrange("p (h s d) -> p h s d", s=2, d=32)[:, :, 0, :],
                            op=ALU.mult),
                          reads=(scr, rott), writes=(scr2,))
                        E("pool", lambda e: e.tensor_tensor(
                            out=t2_.rearrange("p (h s d) -> p h s d", s=2, d=32)[:, :, 1, :],
                            in0=qn2_.rearrange("p (h s d) -> p h s d", s=2, d=32)[:, :, 0, :],
                            in1=rott[:, 512:1024].rearrange("p (h s d) -> p h s d", s=2, d=32)[:, :, 1, :],
                            op=ALU.mult),
                          reads=(scr, rott), writes=(scr2,))
                        E("pool", lambda e: e.tensor_tensor(out=qrot[:, :], in0=t1_, in1=t2_,
                                                            op=ALU.add),
                          reads=(scr2,), writes=(qrot,))
                        dstl = QTa if which == "q" else KTa
                        for jb in range(4):
                            E("pe", lambda e, jb=jb: e.matmul(psT[:, :], qrot[:, jb * 128:(jb + 1) * 128],
                                                              i128b[:, :], start=True, stop=True),
                              reads=(qrot, i128b), writes=(psT,))
                            E("act", lambda e, t=t, jb=jb, dstl=dstl:
                              e.copy(out=dstl[2 * jb][0:64, t * 128:(t + 1) * 128], in_=psT[0:64, :]),
                              reads=(psT,), writes=(dstl[2 * jb],))
                            E("act", lambda e, t=t, jb=jb, dstl=dstl:
                              e.copy(out=dstl[2 * jb + 1][0:64, t * 128:(t + 1) * 128], in_=psT[64:128, :]),
                              reads=(psT,), writes=(dstl[2 * jb + 1],))
            for h in range(8):
                for t in range(16):
                    klo = max(0, t - 4) * 128
                    wk = (t + 1) * 128 - klo
                    base = 640 - wk
                    p1 = min(512, wk); p2 = wk - p1
                    mm(psA[:, 0:p1], QTa[h], QTa[h][0:64, t * 128:(t + 1) * 128],
                       KTa[h], KTa[h][0:64, klo: klo + p1], psA, True, True)
                    E("dve", lambda e, base=base, p1=p1:
                      e.tensor_tensor(out=psA[:, 0:p1], in0=psA[:, 0:p1],
                                      in1=fmask[:, base:base + p1], op=ALU.add),
                      reads=(psA, fmask), writes=(psA,))
                    E("dve", lambda e, p1=p1: e.reduce_max(out=mcol[:, :], in_=psA[:, 0:p1], axis=AX.X),
                      reads=(psA,), writes=(mcol,))
                    if p2 > 0:
                        mm(psB[:, 0:p2], QTa[h], QTa[h][0:64, t * 128:(t + 1) * 128],
                           KTa[h], KTa[h][0:64, klo + p1: klo + wk], psB, True, True)
                        E("dve", lambda e, base=base, p1=p1, p2=p2:
                          e.tensor_tensor(out=psB[:, 0:p2], in0=psB[:, 0:p2],
                                          in1=fmask[:, base + p1:base + wk], op=ALU.add),
                          reads=(psB, fmask), writes=(psB,))
                        E("dve", lambda e, p2=p2: e.reduce_max(out=mcol2[:, :], in_=psB[:, 0:p2], axis=AX.X),
                          reads=(psB,), writes=(mcol2,))
                        E("dve", lambda e: e.tensor_max(mcol[:, :], mcol[:, :], mcol2[:, :]),
                          reads=(mcol, mcol2), writes=(mcol,))
                    E("pe", lambda e: e.matmul(psBC[0:1, 0:128], mcol[:, :], i128b[:, :],
                                               start=True, stop=True),
                      reads=(mcol, i128b), writes=(psBC,))
                    E("act", lambda e, t=t, h=h: e.activation(out=QTa[h][64:65, t * 128:(t + 1) * 128],
                                                              in_=psBC[0:1, 0:128], func=AF.Copy,
                                                              scale=-1.0),
                      reads=(psBC,), writes=(QTa[h],))
                for kt in range(16):
                    wq = min(640, 2048 - kt * 128)
                    p1 = min(512, wq); p2 = wq - p1
                    for c0, w, ps in ((0, p1, psA), (p1, p2, psB)):
                        if w <= 0: continue
                        mm(ps[:, 0:w], KTa[h], KTa[h][0:65, kt * 128:(kt + 1) * 128],
                           QTa[h], QTa[h][0:65, kt * 128 + c0: kt * 128 + c0 + w], ps, True, True)
                    E("dve", lambda e: e.tensor_tensor(out=psA[:, 0:128], in0=psA[:, 0:128],
                                                       in1=smask[:, 0:128], op=ALU.add),
                      reads=(psA, smask), writes=(psA,))
                    if p2 > 0:
                        E("dve", lambda e, p2=p2: e.tensor_tensor(out=psB[:, 0:p2], in0=psB[:, 0:p2],
                                                                  in1=smask[:, 512:512 + p2], op=ALU.add),
                          reads=(psB, smask), writes=(psB,))
                    b0 = (kt * 128) // 512
                    nb = (kt * 128 + wq - 1) // 512 - b0 + 1
                    off = kt * 128 - b0 * 512
                    if off > 0:
                        E("dve", lambda e, off=off: e.memset(PTb[:, 0:off], 0.0), writes=(PTb,))
                    if off + wq < nb * 512:
                        E("dve", lambda e, off=off, wq=wq, nb=nb:
                          e.memset(PTb[:, off + wq: nb * 512], 0.0), writes=(PTb,))
                    E("act", lambda e, off=off, p1=p1: e.activation(out=PTb[:, off:off + p1],
                                                                    in_=psA[:, 0:p1],
                                                                    func=AF.Exp, scale=8.0),
                      reads=(psA,), writes=(PTb,))
                    if p2 > 0:
                        E("act", lambda e, off=off, p1=p1, p2=p2:
                          e.activation(out=PTb[:, off + p1: off + p1 + p2], in_=psB[:, 0:p2],
                                       func=AF.Exp, scale=8.0),
                          reads=(psB,), writes=(PTb,))
                    for i in range(nb):
                        bank = b0 + i
                        stt = (kt == max(0, 4 * bank - 4))
                        stp = (kt == min(15, 4 * bank + 3))
                        E("pe", lambda e, kt=kt, i=i, bank=bank, h=h, stt=stt, stp=stp:
                          e.matmul(psAT[0:65, bank * 512:(bank + 1) * 512],
                                   VA[kt][:, h * 65:(h + 1) * 65],
                                   PTb[:, i * 512:(i + 1) * 512], start=stt, stop=stp),
                          reads=(VA[kt], PTb), writes=(psAT,))
                for band in range(4):
                    E("dve", lambda e, band=band:
                      e.reciprocal(out=rrow[0:1, :], in_=psAT[64:65, band * 512:(band + 1) * 512]),
                      reads=(psAT,), writes=(rrow,))
                    E("act", lambda e, band=band:
                      e.copy(out=asb[0:64, :], in_=psAT[0:64, band * 512:(band + 1) * 512]),
                      reads=(psAT,), writes=(asb,))
                    E("pe", lambda e: e.matmul(psBC[:, :], onesf[:, :], rrow[0:1, :],
                                               start=True, stop=True),
                      reads=(rrow, onesf), writes=(psBC,))
                    jt = h // 2; r0 = (h % 2) * 64
                    E("dve", lambda e, band=band, jt=jt, r0=r0:
                      e.tensor_tensor(out=AT[jt][r0:r0 + 64, band * 512:(band + 1) * 512],
                                      in0=asb[0:64, :], in1=psBC[0:64, :], op=ALU.mult),
                      reads=(asb, psBC), writes=(AT[jt],))
            for j in range(4):
                E("sync", lambda e, j=j, l=l:
                  e.dma_start(out=wkt[:, j * 1024:(j + 1) * 1024],
                              in_=WB[f"f_wo{l}"][j * 128:(j + 1) * 128, :]),
                  reads=(dbuf[f"wb_f_wo{l}"],), writes=(wkt,), dma=True)
            for t in range(16):
                for half in range(2):
                    ps = psA if half == 0 else psB
                    for j in range(4):
                        mm(ps[:, :], AT[j], AT[j][:, t * 128:(t + 1) * 128],
                           wkt, wkt[:, j * 1024 + half * 512: j * 1024 + (half + 1) * 512],
                           ps, j == 0, j == 3)
                dma_in(yt0, yt0[:, :], "yd", yd[t * 128:(t + 1) * 128, :])
                E("dve", lambda e: e.tensor_tensor(out=yt0[:, 0:512], in0=yt0[:, 0:512],
                                                   in1=psA[:, :], op=ALU.add),
                  reads=(yt0, psA), writes=(yt0,))
                E("dve", lambda e: e.tensor_tensor(out=yt0[:, 512:1024], in0=yt0[:, 512:1024],
                                                   in1=psB[:, :], op=ALU.add),
                  reads=(yt0, psB), writes=(yt0,))
                dma_out("pool", "yd", yd[t * 128:(t + 1) * 128, :], yt0, yt0[:, :])
            ctx.barrier()
        with contextlib.ExitStack() as fst:
            hT = [ctx.sbuf(fst, [128, 2048], BF16) for _ in range(8)]
            acts = [ctx.sbuf(fst, [128, 2048], BF16) for _ in range(11)]
            w2r = [ctx.sbuf(fst, [128, 2048], BF16) for _ in range(11)]
            w1t = ctx.sbuf(fst, [128, 2048], BF16)
            gg = ctx.sbuf(fst, [128, 1024], BF16)
            aa = ctx.sbuf(fst, [128, 1024], BF16)
            for i in range(11):
                E("sync", lambda e, i=i, l=l:
                  e.dma_start(out=w2r[i][:, :],
                              in_=WB[f"f_w2{l}"][i * 256:(i + 1) * 256, :]
                              .rearrange("(c p) n -> p c n", p=128)),
                  reads=(dbuf[f"wb_f_w2{l}"],), writes=(w2r[i],), dma=True)
            for t in range(16):
                dma_in(xt0, xt0[:, 0:512], "yd", yd[t * 128:(t + 1) * 128, 0:512])
                if t == 0:
                    E("dve", lambda e: e.memset(xt0[0:1, 512:1024], 0.0), writes=(xt0,))
                    dma_in(xt0, xt0[1:128, 512:1024], "yd", yd[0:127, 512:1024])
                else:
                    dma_in(xt0, xt0[:, 512:1024], "yd", yd[t * 128 - 1:(t + 1) * 128 - 1, 512:1024])
                ln_tile(xt0)
                normed(xt1, xt0)
                E("dve", lambda e: e.tensor_tensor(out=lnt[:, :], in0=xt1[:, :], in1=lgb[l][:, :],
                                                   op=ALU.mult),
                  reads=(xt1, lgb[l]), writes=(lnt,))
                for c in range(8):
                    transpose_block(hT[c], hT[c][:, t * 128:(t + 1) * 128],
                                    lnt, lnt[:, c * 128:(c + 1) * 128], i128b)
            for th in range(2):
                tok0 = th * 1024
                for hc in range(22):
                    E("sync", lambda e, hc=hc, l=l:
                      e.dma_start(out=w1t[:, 0:1024],
                                  in_=WB[f"f_w1a{l}"][:, hc * 128:(hc + 1) * 128]
                                  .rearrange("(c p) n -> p c n", p=128)),
                      reads=(dbuf[f"wb_f_w1a{l}"],), writes=(w1t,), dma=True)
                    E("sync", lambda e, hc=hc, l=l:
                      e.dma_start(out=w1t[:, 1024:2048],
                                  in_=WB[f"f_w1g{l}"][:, hc * 128:(hc + 1) * 128]
                                  .rearrange("(c p) n -> p c n", p=128)),
                      reads=(dbuf[f"wb_f_w1g{l}"],), writes=(w1t,), dma=True)
                    for s in range(2):
                        psX = psA if s == 0 else psB
                        for c in range(8):
                            mm(psX[:, :], w1t, w1t[:, c * 128:(c + 1) * 128],
                               hT[c], hT[c][:, tok0 + s * 512: tok0 + s * 512 + 512],
                               psX, c == 0, c == 7)
                        for c in range(8):
                            mm(psAT[:, s * 512:(s + 1) * 512], w1t, w1t[:, 1024 + c * 128: 1024 + (c + 1) * 128],
                               hT[c], hT[c][:, tok0 + s * 512: tok0 + s * 512 + 512],
                               psAT, c == 0, c == 7)
                    E("act", lambda e, hc=hc: e.activation(out=gg[:, :], in_=psAT[:, 0:1024], func=AF.Gelu,
                                                           bias=b1eg[l][:, hc:hc + 1]),
                      reads=(psAT, b1eg[l]), writes=(gg,))
                    E("act", lambda e, hc=hc: e.activation(out=aa[:, 0:512], in_=psA[:, :], func=AF.Identity,
                                                           bias=b1ea[l][:, hc:hc + 1]),
                      reads=(psA, b1ea[l]), writes=(aa,))
                    E("act", lambda e, hc=hc: e.activation(out=aa[:, 512:1024], in_=psB[:, :], func=AF.Identity,
                                                           bias=b1ea[l][:, hc:hc + 1]),
                      reads=(psB, b1ea[l]), writes=(aa,))
                    E("pool", lambda e, hc=hc: e.tensor_tensor(
                        out=acts[hc // 2][:, (hc % 2) * 1024:(hc % 2 + 1) * 1024],
                        in0=aa[:, :], in1=gg[:, :], op=ALU.mult),
                      reads=(aa, gg), writes=(acts[hc // 2],))
                for tt in range(8):
                    t = th * 8 + tt
                    for half in range(2):
                        ps = psA if half == 0 else psB
                        for hc in range(22):
                            mm(ps[:, :], acts[hc // 2],
                               acts[hc // 2][:, (hc % 2) * 1024 + tt * 128: (hc % 2) * 1024 + (tt + 1) * 128],
                               w2r[hc // 2], w2r[hc // 2][:, (hc % 2) * 1024 + half * 512:
                                                          (hc % 2) * 1024 + half * 512 + 512],
                               ps, hc == 0, False)
                        E("pe", lambda e, half=half, ps=ps, l=l:
                          e.matmul(ps[:, :], onesb[:, :], fb2[l][0:1, half * 512:(half + 1) * 512],
                                   start=False, stop=True),
                          reads=(onesb, fb2[l]), writes=(ps,))
                    dma_in(yt0, yt0[:, :], "yd", yd[t * 128:(t + 1) * 128, :])
                    E("dve", lambda e: e.tensor_tensor(out=yt0[:, 0:512], in0=yt0[:, 0:512],
                                                       in1=psA[:, :], op=ALU.add),
                      reads=(yt0, psA), writes=(yt0,))
                    E("dve", lambda e: e.tensor_tensor(out=yt0[:, 512:1024], in0=yt0[:, 512:1024],
                                                       in1=psB[:, :], op=ALU.add),
                      reads=(yt0, psB), writes=(yt0,))
                    dma_out("pool", "yd", yd[t * 128:(t + 1) * 128, :], yt0, yt0[:, :])
        ctx.barrier()

    def ffn_epi(b0, xin_key, xout_key):
        row0 = b0 * 2048
        for t in range(16):
            if xin_key == 0:
                dma_in(xt0, xt0[:, :], "xd0", XD[0][t * 128:(t + 1) * 128, :])
            else:
                dma_in(xt0, xt0[:, :], "xd1", XD[1][t * 128:(t + 1) * 128, :])
            dma_in(yt0, yt0[:, :], "yd", yd[t * 128:(t + 1) * 128, :])
            E("dve", lambda e: e.tensor_tensor(out=xt0[:, :], in0=xt0[:, :], in1=yt0[:, :], op=ALU.add),
              reads=(xt0, yt0), writes=(xt0,))
            if xout_key is None:
                dma_out("pool", "outp", outp[row0 + t * 128: row0 + (t + 1) * 128, :], xt0, xt0[:, :])
            else:
                dma_out("pool", f"xd{xout_key}", XD[xout_key][t * 128:(t + 1) * 128, :], xt0, xt0[:, :])
                ln_tile(xt0)
                normed(yt0, xt0)
                dma_out("act", "yd", yd[t * 128:(t + 1) * 128, :], yt0, yt0[:, :])
        ctx.barrier()

    def phase_C(b0):
        with contextlib.ExitStack() as cst:
            a2T = [ctx.sbuf(cst, [128, 2048], BF16) for _ in range(8)]
            HF = [ctx.sbuf(cst, [128, 2048], BF16) for _ in range(8)]
            with contextlib.ExitStack() as cst2:
                a2Tr = [ctx.sbuf(cst2, [128, 2048], BF16) for _ in range(8)]
                Asc = ctx.sbuf(cst2, [128, 2048])
                Bsc = ctx.sbuf(cst2, [128, 2048])
                rsc = ctx.sbuf(cst2, [128, 2048])
                hsc = ctx.sbuf(cst2, [128, 2048])
                hbb = ctx.sbuf(cst2, [128, 2048], BF16)
                tmpT = ctx.sbuf(cst2, [128, 128], BF16)
                wkt = ctx.sbuf(cst2, [128, 2048], BF16)
                for t in range(16):
                    dma_in(yt0, yt0[:, :], "yd", yd[t * 128:(t + 1) * 128, :])
                    E("act", lambda e: e.copy(out=lnt[:, :], in_=yt0[:, :]), reads=(yt0,), writes=(lnt,))
                    for c in range(8):
                        transpose_block(a2T[c], a2T[c][:, t * 128:(t + 1) * 128],
                                        lnt, lnt[:, c * 128:(c + 1) * 128], i128b)
                        transpose_block(a2Tr[c], a2Tr[c][:, (15 - t) * 128:(16 - t) * 128],
                                        lnt, lnt[:, c * 128:(c + 1) * 128], j128b)
                HB = a2T
                for d, src, rev in (("f", a2T, False), ("b", a2Tr, True)):
                    for ch in range(8):
                        E("sync", lambda e, ch=ch, d=d:
                          e.dma_start(out=wkt[:, 0:1024],
                                      in_=WB[f"g_wz_{d}"][:, ch * 128:(ch + 1) * 128]
                                      .rearrange("(c p) n -> p c n", p=128)),
                          reads=(dbuf[f"wb_g_wz_{d}"],), writes=(wkt,), dma=True)
                        E("sync", lambda e, ch=ch, d=d:
                          e.dma_start(out=wkt[:, 1024:2048],
                                      in_=WB[f"g_wh_{d}"][:, ch * 128:(ch + 1) * 128]
                                      .rearrange("(c p) n -> p c n", p=128)),
                          reads=(dbuf[f"wb_g_wh_{d}"],), writes=(wkt,), dma=True)
                        for s in range(4):
                            for c in range(8):
                                mm(psA[:, :], wkt, wkt[:, c * 128:(c + 1) * 128],
                                   src[c], src[c][:, s * 512:(s + 1) * 512], psA, c == 0, c == 7)
                            E("act", lambda e, s=s, d=d, ch=ch:
                              e.activation(out=Asc[:, s * 512:(s + 1) * 512], in_=psA[:, :],
                                           func=AF.Sigmoid, scale=-1.0, bias=gbzn[d][:, ch:ch + 1]),
                              reads=(psA, gbzn[d]), writes=(Asc,))
                            for c in range(8):
                                mm(psB[:, :], wkt, wkt[:, 1024 + c * 128: 1024 + (c + 1) * 128],
                                   src[c], src[c][:, s * 512:(s + 1) * 512], psB, c == 0, c == 7)
                            E("act", lambda e, s=s, d=d, ch=ch:
                              e.activation(out=rsc[:, s * 512:(s + 1) * 512], in_=psB[:, :],
                                           func=AF.Relu, bias=gbh[d][:, ch:ch + 1]),
                              reads=(psB, gbh[d]), writes=(rsc,))
                            E("act", lambda e, s=s, d=d, ch=ch:
                              e.activation(out=hsc[:, s * 512:(s + 1) * 512], in_=psB[:, :],
                                           func=AF.Sigmoid, bias=gbh[d][:, ch:ch + 1]),
                              reads=(psB, gbh[d]), writes=(hsc,))
                        E("dve", lambda e: e.scalar_tensor_tensor(out=hsc[:, :], in0=hsc[:, :], scalar=0.5,
                                                                  in1=rsc[:, :], op0=ALU.min, op1=ALU.add),
                          reads=(hsc, rsc), writes=(hsc,))
                        E("dve", lambda e: e.tensor_scalar(out=Bsc[:, :], in0=Asc[:, :], scalar1=-1.0,
                                                           scalar2=1.0, op0=ALU.mult, op1=ALU.add),
                          reads=(Asc,), writes=(Bsc,))
                        E("dve", lambda e: e.tensor_tensor(out=Bsc[:, :], in0=Bsc[:, :], in1=hsc[:, :],
                                                           op=ALU.mult),
                          reads=(Bsc, hsc), writes=(Bsc,))
                        E("dve", lambda e: e.tensor_tensor_scan(out=hsc[:, :], data0=Asc[:, :],
                                                                data1=Bsc[:, :], initial=0.0,
                                                                op0=ALU.mult, op1=ALU.add),
                          reads=(Asc, Bsc), writes=(hsc,))
                        if not rev:
                            E("act", lambda e, ch=ch: e.copy(out=HF[ch][:, :], in_=hsc[:, :]),
                              reads=(hsc,), writes=(HF[ch],))
                        else:
                            E("act", lambda e, ch=ch: e.copy(out=HB[ch][:, :], in_=hsc[:, :]),
                              reads=(hsc,), writes=(HB[ch],))
                ctx.barrier()
            with contextlib.ExitStack() as cst3:
                wkf = ctx.sbuf(cst3, [128, 8192], BF16)
                wkb = ctx.sbuf(cst3, [128, 8192], BF16)
                for j in range(8):
                    E("sync", lambda e, j=j: e.dma_start(out=wkf[:, j * 1024:(j + 1) * 1024],
                                                         in_=WB["g_wo"][j * 128:(j + 1) * 128, :]),
                      reads=(dbuf["wb_g_wo"],), writes=(wkf,), dma=True)
                    E("sync", lambda e, j=j: e.dma_start(out=wkb[:, j * 1024:(j + 1) * 1024],
                                                         in_=WB["g_wo"][1024 + j * 128: 1024 + (j + 1) * 128, :]),
                      reads=(dbuf["wb_g_wo"],), writes=(wkb,), dma=True)
                for t in range(16):
                    for half in range(2):
                        ps = psA if half == 0 else psB
                        for ch in range(8):
                            mm(ps[:, :], HF[ch], HF[ch][:, t * 128:(t + 1) * 128],
                               wkf, wkf[:, ch * 1024 + half * 512: ch * 1024 + (half + 1) * 512],
                               ps, ch == 0, False)
                        for ch in range(8):
                            mm(ps[:, :], HB[ch], HB[ch][:, t * 128:(t + 1) * 128],
                               wkb, wkb[:, ch * 1024 + half * 512: ch * 1024 + (half + 1) * 512],
                               ps, False, False)
                        E("pe", lambda e, half=half, ps=ps:
                          e.matmul(ps[:, :], onesb[:, :], gbo[0:1, half * 512:(half + 1) * 512],
                                   start=False, stop=True),
                          reads=(onesb, gbo), writes=(ps,))
                    dma_in(xt0, xt0[:, :], "xd1", XD[1][t * 128:(t + 1) * 128, :])
                    E("dve", lambda e: e.tensor_tensor(out=xt0[:, 0:512], in0=xt0[:, 0:512],
                                                       in1=psA[:, :], op=ALU.add),
                      reads=(xt0, psA), writes=(xt0,))
                    E("dve", lambda e: e.tensor_tensor(out=xt0[:, 512:1024], in0=xt0[:, 512:1024],
                                                       in1=psB[:, :], op=ALU.add),
                      reads=(xt0, psB), writes=(xt0,))
                    dma_out("pool", "xd0", XD[0][t * 128:(t + 1) * 128, :], xt0, xt0[:, :])
                    ln_tile(xt0)
                    normed(yt0, xt0)
                    dma_out("act", "yd", yd[t * 128:(t + 1) * 128, :], yt0, yt0[:, :])
        ctx.barrier()

    if dbg is not None:
        phase_A(0)
        if dbg >= 1:
            for l in range(2):
                ffn_layer(l)
            ffn_epi(0, 0, 1)
        if dbg >= 2:
            phase_C(0)
        if dbg >= 3:
            for l in range(2):
                ffn_layer(l)
            ffn_epi(0, 0, None)
            ctx.finalize()
            gst.close()
            return nc
        for t in range(16):
            src_ = {0: XD[0], 1: XD[1], 2: XD[0]}[dbg]
            key_ = {0: "xd0", 1: "xd1", 2: "xd0"}[dbg]
            dma_in(xt0, xt0[:, :], key_, src_[t * 128:(t + 1) * 128, :])
            dma_out("pool", "outp", outp[t * 128:(t + 1) * 128, :], xt0, xt0[:, :])
        ctx.finalize()
        gst.close()
        return nc
    for b0 in range(2):
        phase_A(b0)
        for l in range(2):
            ffn_layer(l)
        ffn_epi(b0, 0, 1)
        phase_C(b0)
        for l in range(2):
            ffn_layer(l)
        ffn_epi(b0, 0, None)

    ctx.finalize()
    gst.close()
    return nc

def kernel(**inp):
    consts = host_consts()
    w = host_weights(inp)
    x = np.ascontiguousarray(np.asarray(inp["L0_attn_a_x"], np.float32))
    nc = bass.Bass("TRN2", target_bir_lowering=False)
    build(nc)
    base = {}
    base.update(consts)
    base.update(w)
    in_maps = []
    for i in range(8):
        m = dict(base)
        m["xin"] = np.ascontiguousarray(x[2 * i:2 * i + 2].reshape(4096, 1024))
        in_maps.append(m)
    res = run_bass_kernel_spmd(nc, in_maps, core_ids=list(range(8)))
    out = np.concatenate([np.asarray(r["out"]).reshape(2, 2048, 1024) for r in res.results], axis=0)
    return out.astype(np.float32)
